# version 12
# baseline (speedup 1.0000x reference)
import contextlib
import numpy as np

import concourse.bass as bass
import concourse.mybir as mybir
from concourse.bass_utils import run_bass_kernel_spmd
from concourse.bass_core_rust import IndirectOffsetOnAxis

F32 = mybir.dt.float32
AF = mybir.ActivationFunctionType
ALU = mybir.AluOpType
AX = mybir.AxisListType

NCORES = 8
D = 1024
DI = 2048
NST = 16
TC = 2048
TS = 512
NCH = TC // TS
NKD = D // 128
NTI = DI // 128
NE = 8
EPS = 1e-5
NG = 4
TSLOT = 256
NTILES = 24
SDUMP = NTILES * TSLOT
STBL = SDUMP + 128
GROUPS = [[0, 1, 2, 3, 4, 5, 6, 7]]
PROJ_GRPS = [[12, 13, 14, 15, 16], [0, 1, 2, 3, 4, 5], [6, 7, 8, 9, 10, 11]]


class Sem:
    def __init__(self, nc, ctx, name):
        self.h = ctx.enter_context(nc.semaphore(name))
        self.n = 0


DEBUG = False


def build_program():
    nc = bass.Bass("TRN2", target_bir_lowering=False)

    def din(name, shape, dt=F32):
        return nc.dram_tensor(name, shape, dt, kind="ExternalInput").ap()

    x0_h = din("x0", [D, TC])
    x0ext_h = din("x0ext", [D, 3])
    win_h = [din(f"win{l}", [32, 128, NKD * 128]) for l in range(2)]
    wx0_h = din("wx0", [128, NTI * 128])
    wdt0_h = din("wdt0", [16, 64, 128])
    wx1_h = din("wx1", [16, 17, 128, 128])
    wout_h = [din(f"wout{l}", [8, 128, NTI * 128]) for l in range(2)]
    convw_h = [din(f"convw{l}", [128, NTI * 4]) for l in range(2)]
    convb_h = [din(f"convb{l}", [128, NTI]) for l in range(2)]
    alog_h = [din(f"alog{l}", [128, NTI * NST]) for l in range(2)]
    dskip_h = [din(f"dskip{l}", [128, NTI]) for l in range(2)]
    bdt0_h = din("bdt0", [128, NTI])
    bin1_h = din("bin1", [128, 32])
    bx1_h = din("bx1", [128, 17])
    bxbc1_h = din("bxbc1", [128, 1])
    bout1_h = din("bout1", [128, 8])
    ffnw_h = din("ffnw", [NE, 8, 128, NKD * 128])
    ffnb_h = din("ffnb", [128, NE * 8])
    logits_h = din("logits", [1, NE])
    gumu_h = din("gumu", [1, NE])
    w1_h = din("w1", [NE, 32, 128, NKD * 128])
    w2_h = din("w2", [NE, 8, 2, 128, 16 * 128])
    b1_h = din("b1", [128, NE * 32])
    b2_h = din("b2", [NE, 1024])
    gatew_h = din("gatew", [128, NKD * 8])
    bcsel_h = din("bcsel", [NST, NST * 128])
    predm_h = din("predm", [128, NCORES])
    cmt_h = din("cmt", [128, NCORES * NCORES])
    halom_h = din("halom", [128, 1])
    eye_h = din("eye", [128, 128])
    ltri_h = din("ltri", [16, 128, 2048])
    iota_h = din("iota", [128, 1])
    tokid_h = din("tokid", [128, 16])
    tstart_h = din("tstart", [1, 32])
    b1t_h = din("b1t", [NE * 32, 128])
    cvec_h = din("cvec", [128, 5])

    out_h = nc.dram_tensor("o", [TC, D], F32, kind="ExternalOutput").ap()

    KIND = "ExternalOutput" if DEBUG else "Internal"
    x1_h = nc.dram_tensor("x1", [D, TC], F32, kind=KIND).ap()
    spy_h = nc.dram_tensor("spy", [DI, TC], F32, kind=KIND).ap()
    spcu_h = nc.dram_tensor("spcu", [DI, TC], F32, kind=KIND).ap()
    spz_h = nc.dram_tensor("spz", [DI, TC], F32, kind=KIND).ap()
    spxc_h = nc.dram_tensor("spxc", [DI, TC], F32, kind=KIND).ap()
    spC_h = nc.dram_tensor("spC", [NST, TC], F32, kind=KIND).ap()
    wsel_h = nc.dram_tensor("wselh", [128, NKD * 1024], F32, kind=KIND).ap()
    cc_in = [nc.dram_tensor(f"ccin{l}", [128, 272], F32, kind="Internal").ap()
             for l in range(2)]
    cc_out = [nc.dram_tensor(f"ccout{l}", [NCORES, 128, 272], F32,
                             kind="Internal").ap() for l in range(2)]
    cc2_in = nc.dram_tensor("cc2in", [128, 24], F32, kind="Internal").ap()
    if DEBUG:
        dbg_gl = nc.dram_tensor("dbg_gl", [NE, TC], F32, kind="ExternalOutput").ap()
        dbg_cmb = nc.dram_tensor("dbg_cmb", [NE, TC], F32, kind="ExternalOutput").ap()
        dbg_h = nc.dram_tensor("dbg_h", [2 * 2048, TS], F32, kind="ExternalOutput").ap()
        dbg_xacc = nc.dram_tensor("dbg_xacc", [D, TC], F32, kind="ExternalOutput").ap()
        dbg_sm = nc.dram_tensor("dbg_sm", [128, 160], F32, kind="ExternalOutput").ap()
    cc2_out = nc.dram_tensor("cc2out", [NCORES, 128, 24], F32, kind="Internal").ap()
    xnt_h = nc.dram_tensor("xnt", [TC, 1024], F32, kind=KIND).ap()
    stok_h = nc.dram_tensor("stok", [STBL, 1], F32, kind=KIND).ap()
    scmb_h = nc.dram_tensor("scmb", [STBL, 1], F32, kind=KIND).ap()

    ctx = contextlib.ExitStack()
    E = ctx.enter_context

    def sb(name, shape, dt=F32):
        return E(nc.sbuf_tensor(name, shape, dt))

    xn = sb("xn", [128, NKD * TS])
    xa = [sb(f"xa{i}", [128, TS]) for i in range(2)]
    sqt = sb("sqt", [128, TS])
    stat = sb("stat", [1, 3 * TS + 16])
    xnh = sb("xnh", [128, NKD * 3])
    wbuf = [sb(f"wbuf{i}", [128, 2048]) for i in range(2)]
    wdtb = [sb(f"wdtb{i}", [64, 128]) for i in range(2)]
    xr = sb("xr", [128, TS + 3])
    xrc = sb("xrc", [128, NTI * 3])
    cacc = [sb(f"cacc{i}", [128, TS]) for i in range(2)]
    xcr = [sb(f"xcr{i}", [128, TS]) for i in range(2)]
    dtt = sb("dtt", [128, TS])
    zbuf = sb("zbuf", [128, TS])
    dtlr = zbuf
    praw = sb("praw", [128, TS])
    pab = sb("pab", [128, TS])
    Bsb = sb("Bsb", [NST, TS])
    Csb = sb("Csb", [NST, TS])
    dtx = sb("dtx", [128, TS])
    ysl = sb("ysl", [128, TS])
    ytmp = sb("ytmp", [128, TS])
    cum = sb("cum", [128, TS])
    hcar = sb("hcar", [128, NTI * NST])
    ccar = sb("ccar", [128, NTI])
    asb = sb("asb", [128, NTI * NST])
    convw = sb("convw", [128, NTI * 4])
    convb = sb("convb", [128, NTI])
    dskip = sb("dskip", [128, NTI])
    bdt = sb("bdt", [128, NTI])
    bin1 = sb("bin1s", [128, 32])
    bx1 = sb("bx1s", [128, 17])
    bxbc = sb("bxbc", [128, 1])
    bout1 = sb("bout1s", [128, 8])
    onesc = sb("onesc", [128, 1])
    onesr = sb("onesr", [1, 128])
    eye = sb("eye_s", [128, 128])
    cvec = sb("cvec_s", [128, 5])
    bcsel = sb("bcsel_s", [NST, NST * 128])
    gsb = sb("gsb", [128, NCORES * 272])
    dtq = gsb
    wxbuf = gsb
    g2sb = sb("g2sb", [128, 24])
    sssb = sb("sssb", [128, NCORES * NTI])
    hin = sb("hin", [128, NTI * NST])
    predm = sb("predm_s", [128, NCORES])
    cmt = sb("cmt_s", [128, NCORES * NCORES])
    halom = sb("halom_s", [128, 1])
    etile = sb("etile", [128, NST])
    ttile = sb("ttile", [128, NST])
    szb = sb("szb", [128, TS])
    bsel = sb("bsel", [128, 8])
    masksb = sb("masksb", [128, NE])
    srow = sb("srow", [1, NE])
    trow = sb("trow", [1, NE])
    mrow = sb("mrow", [1, 1])
    gtok = sb("gtok", [128, (TS // 128) * NE])
    selm = sb("selm", [128, 128])
    cmbm = sb("cmbm", [128, 128])
    rankm = sb("rankm", [128, 128])
    posi = sb("posi", [128, 128], mybir.dt.int32)
    itmp = sb("itmp", [1, 32], mybir.dt.int32)
    idxi = sb("idxi", [128, 8], mybir.dt.int32)
    exall = sb("exall", [128, 32])
    iotap = sb("iotap", [128, 1])
    tokid = sb("tokid_s", [128, 16])
    tstart = sb("tstart_s", [1, 32])
    tkf = sb("tkf", [128, 2])
    b1g = sb("b1g", [32, 128])
    b1x = sb("b1x", [128, 32])
    gfm = sb("gfm", [NE, TS])
    b2sb = sb("b2sb", [NE, 1024])
    gwsb = sb("gwsb", [128, NKD * 8])
    tmp8 = sb("tmp8", [128, TS])
    scr = sb("scr", [128, TS])

    AR1 = sb("AR1", [128, NST * TS])
    AR2 = sb("AR2", [128, NST * TS])
    AR3 = sb("AR3", [128, 3 * NG * TS])

    Brep, Crep, ygb, hsb = AR1, AR2, AR1, AR1
    CREPE = NKD * TS

    psum = [E(nc.psum_tensor(f"ps{i}", [128, 512], F32)) for i in range(8)]

    LD = Sem(nc, ctx, "LD")
    ST = Sem(nc, ctx, "ST")
    TN = Sem(nc, ctx, "TN")
    VC = Sem(nc, ctx, "VC")
    SC = Sem(nc, ctx, "SC")
    GC = Sem(nc, ctx, "GC")
    SEMS = {"LD": LD, "ST": ST, "TN": TN, "VC": VC, "SC": SC, "GC": GC}

    plan = []

    def emit(eng, fn):
        plan.append((eng, fn))

    def wait(eng, *sems):
        snap = [(s.h, s.n) for s in sems if s.n > 0]
        if not snap:
            return

        def fn(e):
            for h, n in snap:
                e.wait_ge(h, n)
        emit(eng, fn)

    def cap(*sems):
        return [(s.h, s.n) for s in sems if s.n > 0]

    REL = {k: [] for k in ("wb0", "wb1", "xc0", "xc1", "wd0", "wd1",
                           "xa0", "xa1")}

    def wait_snap(eng, snap):
        if not snap:
            return

        def fn(e):
            for h, n in snap:
                e.wait_ge(h, n)
        emit(eng, fn)

    bank_rel = [dict() for _ in range(8)]

    def acquire_bank(i):
        snap = [(SEMS[k].h, v) for k, v in bank_rel[i].items()]

        def fn(e):
            for h, n in snap:
                e.wait_ge(h, n)
        emit("tensor", fn)

    def release_bank(i, semname):
        bank_rel[i][semname] = SEMS[semname].n

    def mk_dma(eng_name, sem, dst_fn, src_fn, accum=False):
        def fn(e):
            kw = {}
            if accum:
                kw["accum_op"] = ALU.add
            e.dma_start(out=dst_fn(), in_=src_fn(), **kw).then_inc(sem.h, 16)
        emit(eng_name, fn)
        sem.n += 16

    def mk(eng_name, sem, op_fn, k=1, nop=False):
        def fn(e):
            op_fn(e).then_inc(sem.h, k)
            if eng_name == "scalar":
                e.nop()
            elif nop and eng_name == "vector":
                e.engine_nop()
        emit(eng_name, fn)
        sem.n += k

    for dst, src in ((predm, predm_h), (cmt, cmt_h), (halom, halom_h),
                     (eye, eye_h), (b2sb, b2_h),
                     (gwsb, gatew_h), (bin1, bin1_h), (bx1, bx1_h),
                     (bxbc, bxbc1_h), (bout1, bout1_h), (bdt, bdt0_h),
                     (bcsel, bcsel_h), (cvec, cvec_h), (iotap, iota_h),
                     (tokid, tokid_h), (tstart, tstart_h)):
        mk_dma("sync", LD, (lambda d=dst: d[:]), (lambda s=src: s))
    mk("vector", VC, lambda e: e.memset(onesc[:], 1.0))
    mk("vector", VC, lambda e: e.memset(onesr[:], 1.0))

    mk_dma("sync", LD, lambda: srow[:], lambda: logits_h)
    mk_dma("sync", LD, lambda: trow[:], lambda: gumu_h)
    wait("scalar", LD)
    mk("scalar", SC, lambda e: e.activation(out=trow[:], in_=trow[:], func=AF.Ln))
    wait("vector", SC)
    mk("vector", VC, lambda e: e.tensor_scalar_mul(trow[:], trow[:], -1.0), nop=True)
    wait("scalar", VC)
    mk("scalar", SC, lambda e: e.activation(out=trow[:], in_=trow[:], func=AF.Ln))
    wait("vector", SC)
    mk("vector", VC, lambda e: e.tensor_sub(srow[:], srow[:], trow[:]), nop=True)
    mk("vector", VC, lambda e: e.reduce_max(mrow[:], srow[:], axis=AX.X), nop=True)
    mk("vector", VC, lambda e: e.tensor_scalar(
        out=srow[:], in0=srow[:], scalar1=mrow[0:1, 0:1], scalar2=None,
        op0=ALU.is_ge), nop=True)
    acquire_bank(0)
    wait("tensor", VC, LD)
    mk("tensor", TN, lambda e: e.matmul(psum[0][:, 0:NE], onesr[:], srow[:],
                                        start=True, stop=True))
    wait("vector", TN)
    mk("vector", VC, lambda e: e.tensor_copy(masksb[:], psum[0][:, 0:NE]), nop=True)
    release_bank(0, "VC")
    mk("vector", VC, lambda e: e.memset(AR1[:, 0:8192], 0.0))
    mk("vector", VC, lambda e: e.memset(bsel[:], 0.0), nop=True)
    mk_dma("sync", LD, lambda: AR2[:, 0:NE * 8], lambda: ffnb_h)
    for e_ in range(NE):
        for hh in range(4):
            par = hh % 2
            wait_snap("sync", REL[f"wb{par}"])
            mk_dma("sync", LD, lambda par=par: wbuf[par][:],
                   lambda e_=e_, hh=hh: ffnw_h[e_, hh * 2:(hh + 1) * 2].rearrange(
                       "a p q -> p a q"))
            wait("vector", LD)
            mk("vector", VC, lambda e, e_=e_, hh=hh, par=par: e.scalar_tensor_tensor(
                out=AR1[:, hh * 2048:(hh + 1) * 2048], in0=wbuf[par][:],
                scalar=masksb[:, e_:e_ + 1],
                in1=AR1[:, hh * 2048:(hh + 1) * 2048], op0=ALU.mult, op1=ALU.add))
            REL[f"wb{par}"] = cap(VC)
        mk("vector", VC, lambda e, e_=e_: e.scalar_tensor_tensor(
            out=bsel[:], in0=AR2[:, e_ * 8:(e_ + 1) * 8],
            scalar=masksb[:, e_:e_ + 1],
            in1=bsel[:], op0=ALU.mult, op1=ALU.add), nop=True)
    wait("gpsimd", VC)
    mk_dma("gpsimd", ST, lambda: wsel_h, lambda: AR1[:, 0:8192])

    def ln_stats_to_xn(c, src_h, dst, ncols, src_is_sb=None):
        o_m, o_r, o_t = 0, TS, 2 * TS
        acquire_bank(6)
        acquire_bank(7)
        for k in range(NKD):
            if src_is_sb is None:
                par = k % 2
                wait("sync", ST, TN, SC, VC)
                mk_dma("sync", LD, lambda par=par, ncols=ncols: xa[par][:, 0:ncols],
                       lambda k=k, c=c, ncols=ncols: src_h[
                           k * 128:(k + 1) * 128, c * TS:c * TS + ncols])
                wait("scalar", LD, TN)
                mk("scalar", SC, lambda e, par=par, ncols=ncols: e.activation(
                    out=sqt[:, 0:ncols], in_=xa[par][:, 0:ncols], func=AF.Square))
                srcap = (lambda par=par, ncols=ncols: xa[par][:, 0:ncols])
            else:
                wait("scalar", VC, TN)
                mk("scalar", SC, lambda e, k=k, ncols=ncols: e.activation(
                    out=sqt[:, 0:ncols],
                    in_=src_is_sb[:, k * ncols:(k + 1) * ncols], func=AF.Square))
                srcap = (lambda k=k, ncols=ncols:
                         src_is_sb[:, k * ncols:(k + 1) * ncols])
            wait("tensor", SC)
            mk("tensor", TN, lambda e, srcap=srcap, k=k, ncols=ncols: e.matmul(
                psum[6][0:1, 0:ncols], onesc[:], srcap(),
                start=(k == 0), stop=(k == NKD - 1)))
            mk("tensor", TN, lambda e, k=k, ncols=ncols: e.matmul(
                psum[7][0:1, 0:ncols], onesc[:], sqt[:, 0:ncols],
                start=(k == 0), stop=(k == NKD - 1)))
        wait("vector", TN)
        mk("vector", VC, lambda e, ncols=ncols: e.tensor_scalar_mul(
            stat[0:1, o_m:o_m + ncols], psum[6][0:1, 0:ncols], 1.0 / D), nop=True)
        mk("vector", VC, lambda e, ncols=ncols: e.tensor_scalar_mul(
            stat[0:1, o_t:o_t + ncols], psum[7][0:1, 0:ncols], 1.0 / D), nop=True)
        release_bank(6, "VC")
        release_bank(7, "VC")
        mk("vector", VC, lambda e, ncols=ncols: e.tensor_mul(
            stat[0:1, o_r:o_r + ncols], stat[0:1, o_m:o_m + ncols],
            stat[0:1, o_m:o_m + ncols]), nop=True)
        mk("vector", VC, lambda e, ncols=ncols: e.tensor_sub(
            stat[0:1, o_r:o_r + ncols], stat[0:1, o_t:o_t + ncols],
            stat[0:1, o_r:o_r + ncols]), nop=True)
        wait("scalar", VC)
        mk("scalar", SC, lambda e, ncols=ncols: e.activation(
            out=stat[0:1, o_r:o_r + ncols], in_=stat[0:1, o_r:o_r + ncols],
            func=AF.Ln, bias=cvec[0:1, 0:1]))
        mk("scalar", SC, lambda e, ncols=ncols: e.activation(
            out=stat[0:1, o_r:o_r + ncols], in_=stat[0:1, o_r:o_r + ncols],
            func=AF.Exp, scale=cvec[0:1, 1:2]))
        acquire_bank(4)
        acquire_bank(5)
        wait("tensor", SC)
        mk("tensor", TN, lambda e, ncols=ncols: e.matmul(
            psum[4][:, 0:ncols], onesr[:], stat[0:1, o_m:o_m + ncols],
            start=True, stop=True))
        mk("tensor", TN, lambda e, ncols=ncols: e.matmul(
            psum[5][:, 0:ncols], onesr[:], stat[0:1, o_r:o_r + ncols],
            start=True, stop=True))
        wait("vector", TN)
        for k in range(NKD):
            if src_is_sb is None:
                par = k % 2
                wait("sync", VC, SC)
                mk_dma("sync", LD, lambda par=par, ncols=ncols: xa[par][:, 0:ncols],
                       lambda k=k, c=c, ncols=ncols: src_h[
                           k * 128:(k + 1) * 128, c * TS:c * TS + ncols])
                wait("vector", LD)
                srcap = (lambda par=par, ncols=ncols: xa[par][:, 0:ncols])
            else:
                srcap = (lambda k=k, ncols=ncols:
                         src_is_sb[:, k * ncols:(k + 1) * ncols])
            mk("vector", VC, lambda e, srcap=srcap, k=k, ncols=ncols: e.tensor_sub(
                dst[:, k * ncols:(k + 1) * ncols], srcap(), psum[4][:, 0:ncols]), nop=True)
            mk("vector", VC, lambda e, k=k, ncols=ncols: e.tensor_mul(
                dst[:, k * ncols:(k + 1) * ncols],
                dst[:, k * ncols:(k + 1) * ncols], psum[5][:, 0:ncols]), nop=True)
        release_bank(4, "VC")
        release_bank(5, "VC")

    def softplus_from_psum(bank, bias_ap, out_ap):
        wait("scalar", TN, VC)
        mk("scalar", SC, lambda e, bias_ap=bias_ap: e.activation(
            out=praw[:], in_=psum[bank][:, 0:TS], func=AF.Identity, bias=bias_ap))
        release_bank(bank, "SC")
        mk("scalar", SC, lambda e: e.activation(
            out=pab[:], in_=praw[:], func=AF.Abs))
        mk("scalar", SC, lambda e: e.activation(
            out=pab[:], in_=pab[:], func=AF.Exp, scale=cvec[:, 2:3]))
        mk("scalar", SC, lambda e: e.activation(
            out=pab[:], in_=pab[:], func=AF.Ln, bias=cvec[:, 3:4]))
        wait("vector", SC)
        mk("vector", VC, lambda e: e.scalar_tensor_tensor(
            out=out_ap, in0=praw[:], scalar=0.0, in1=pab[:],
            op0=ALU.max, op1=ALU.add))

    def build_rep(src_sb, dst_arena):
        for n in range(NST):
            bank = 6 + (n % 2)
            acquire_bank(bank)
            wait("tensor", VC, SC)
            mk("tensor", TN, lambda e, bank=bank, n=n: e.matmul(
                psum[bank][:, 0:TS], bcsel[:, n * 128:(n + 1) * 128], src_sb[:],
                start=True, stop=True))
            wait("vector", TN)
            mk("vector", VC, lambda e, bank=bank, n=n: e.tensor_copy(
                dst_arena[:, n * TS:(n + 1) * TS], psum[bank][:, 0:TS]))
            release_bank(bank, "VC")

    def scan_block(m, c):
        par = m % 2
        wait_snap("sync", REL[f"xc{par}"])
        mk_dma("sync", LD, lambda par=par: xcr[par][:],
               lambda m=m, c=c: spxc_h[m * 128:(m + 1) * 128,
                                       c * TS:(c + 1) * TS])
        wait("vector", LD, ST)
        mk("vector", VC, lambda e, par=par: e.tensor_mul(
            dtx[:], dtt[:], xcr[par][:]))
        mk("vector", VC, lambda e, par=par, m=m: e.tensor_scalar(
            out=ysl[:], in0=xcr[par][:], scalar1=dskip[:, m:m + 1],
            scalar2=None, op0=ALU.mult))
        REL[f"xc{par}"] = cap(VC)
        for ng in range(NST // NG):
            wait("scalar", VC)
            for nn in range(NG):
                n = ng * NG + nn
                mk("scalar", SC, lambda e, m=m, n=n, nn=nn: e.activation(
                    out=AR3[:, nn * TS:(nn + 1) * TS], in_=dtt[:],
                    func=AF.Exp, scale=asb[:, m * NST + n:m * NST + n + 1]))
            for nn in range(NG):
                n = ng * NG + nn
                mk("vector", VC, lambda e, n=n, nn=nn: e.tensor_mul(
                    AR3[:, (NG + nn) * TS:(NG + nn + 1) * TS], dtx[:],
                    Brep[:, n * TS:(n + 1) * TS]))
            wait("vector", SC)
            for nn in range(NG):
                n = ng * NG + nn
                ini = (0.0 if c == 0 else None)
                if c == 0:
                    mk("vector", VC, lambda e, nn=nn: e.tensor_tensor_scan(
                        out=AR3[:, (2 * NG + nn) * TS:(2 * NG + nn + 1) * TS],
                        data0=AR3[:, nn * TS:(nn + 1) * TS],
                        data1=AR3[:, (NG + nn) * TS:(NG + nn + 1) * TS],
                        initial=0.0, op0=ALU.mult, op1=ALU.add), nop=True)
                else:
                    mk("vector", VC, lambda e, m=m, n=n, nn=nn: e.tensor_tensor_scan(
                        out=AR3[:, (2 * NG + nn) * TS:(2 * NG + nn + 1) * TS],
                        data0=AR3[:, nn * TS:(nn + 1) * TS],
                        data1=AR3[:, (NG + nn) * TS:(NG + nn + 1) * TS],
                        initial=hcar[:, m * NST + n:m * NST + n + 1],
                        op0=ALU.mult, op1=ALU.add), nop=True)
            for nn in range(NG):
                n = ng * NG + nn
                mk("vector", VC, lambda e, m=m, n=n, nn=nn: e.tensor_copy(
                    hcar[:, m * NST + n:m * NST + n + 1],
                    AR3[:, (2 * NG + nn) * TS + TS - 1:(2 * NG + nn) * TS + TS]), nop=True)
            mk("vector", VC, lambda e, ng=ng: e.tensor_mul(
                AR3[:, 0:NG * TS], AR3[:, 2 * NG * TS:3 * NG * TS],
                Crep[:, ng * NG * TS:(ng + 1) * NG * TS]))
            mk("vector", VC, lambda e: e.tensor_reduce(
                out=ytmp[:], in_=AR3[:, 0:NG * TS].rearrange(
                    "p (n t) -> p t n", n=NG),
                axis=AX.X, op=ALU.add))
            mk("vector", VC, lambda e: e.tensor_add(ysl[:], ysl[:], ytmp[:]))
        if c == 0:
            mk("vector", VC, lambda e: e.tensor_tensor_scan(
                out=cum[:], data0=dtt[:], data1=dtt[:],
                initial=0.0, op0=ALU.add, op1=ALU.bypass), nop=True)
        else:
            mk("vector", VC, lambda e, m=m: e.tensor_tensor_scan(
                out=cum[:], data0=dtt[:], data1=dtt[:],
                initial=ccar[:, m:m + 1], op0=ALU.add, op1=ALU.bypass), nop=True)
        mk("vector", VC, lambda e, m=m: e.tensor_copy(
            ccar[:, m:m + 1], cum[:, TS - 1:TS]), nop=True)
        wait("gpsimd", VC)
        mk_dma("gpsimd", ST,
               lambda m=m, c=c: spy_h[m * 128:(m + 1) * 128, c * TS:(c + 1) * TS],
               lambda: ysl[:])
        mk_dma("gpsimd", ST,
               lambda m=m, c=c: spcu_h[m * 128:(m + 1) * 128, c * TS:(c + 1) * TS],
               lambda: cum[:])

    def ln_halo_from_hbm(src3):
        wait("sync", SC, VC, TN)
        mk_dma("sync", LD, lambda: xnh[:],
               lambda: src3.rearrange("(a p) t -> p a t", p=128))
        wait("vector", LD)
        ln_stats_to_xn(0, None, xnh, 3, src_is_sb=xnh)

    def layer(l):
        is0 = (l == 0)
        src_h = x0_h if is0 else x1_h
        dst_h = x1_h if is0 else out_h

        wait("sync", SC, VC)
        mk_dma("sync", LD, lambda: convw[:], lambda: convw_h[l])
        mk_dma("sync", LD, lambda: convb[:], lambda: convb_h[l])
        mk_dma("sync", LD, lambda: dskip[:], lambda: dskip_h[l])
        mk_dma("sync", LD, lambda: asb[:], lambda: alog_h[l])
        wait("scalar", LD)
        mk("scalar", SC, lambda e: e.activation(out=asb[:], in_=asb[:], func=AF.Exp))
        wait("vector", SC)
        mk("vector", VC, lambda e: e.tensor_scalar_mul(asb[:], asb[:], -1.0))

        for c in range(NCH):
            ln_stats_to_xn(c, src_h, xn, TS)
            if c == 0:
                if is0:
                    ln_halo_from_hbm(x0ext_h)
                else:
                    ln_stats_to_xn(0, None, xnh, 3, src_is_sb=g2sb)
            xnrdy = cap(VC)

            if is0:
                wait("sync", TN)
                mk_dma("sync", LD, lambda: wxbuf[:, 0:NTI * 128], lambda: wx0_h)
                acquire_bank(3)

            for m in range(32):
                par = m % 2
                wait_snap("sync", REL[f"wb{par}"])
                mk_dma("sync", LD, lambda par=par: wbuf[par][:, 0:NKD * 128],
                       lambda m=m: win_h[l][m])
                bank = m % 3
                acquire_bank(bank)
                wait("tensor", LD)
                wait_snap("tensor", xnrdy)
                for k in range(NKD):
                    mk("tensor", TN, lambda e, bank=bank, par=par, k=k: e.matmul(
                        psum[bank][:, 0:TS], wbuf[par][:, k * 128:(k + 1) * 128],
                        xn[:, k * TS:(k + 1) * TS],
                        start=(k == 0), stop=(k == NKD - 1)))
                if c == 0 and m < NTI:
                    hbank = 4 + (m % 2)
                    acquire_bank(hbank)
                    for k in range(NKD):
                        mk("tensor", TN, lambda e, hbank=hbank, par=par, k=k:
                           e.matmul(
                               psum[hbank][:, 0:3],
                               wbuf[par][:, k * 128:(k + 1) * 128],
                               xnh[:, k * 3:(k + 1) * 3],
                               start=(k == 0), stop=(k == NKD - 1)))
                    wait("scalar", TN, VC)
                    hb_ap = ((lambda: cvec[:, 4:5]) if is0
                             else (lambda m=m: bin1[:, m:m + 1]))
                    mk("scalar", SC, lambda e, hbank=hbank, m=m, hb_ap=hb_ap:
                       e.activation(
                           out=xrc[:, m * 3:(m + 1) * 3], in_=psum[hbank][:, 0:3],
                           func=AF.Identity, bias=hb_ap()))
                    release_bank(hbank, "SC")
                    wait("vector", SC)
                    mk("vector", VC, lambda e, m=m: e.tensor_scalar(
                        out=xrc[:, m * 3:(m + 1) * 3],
                        in0=xrc[:, m * 3:(m + 1) * 3],
                        scalar1=halom[:, 0:1], scalar2=None, op0=ALU.mult), nop=True)
                REL[f"wb{par}"] = cap(TN)
                if m < NTI:
                    wait("scalar", TN, VC, ST)
                    xb_ap = ((lambda: cvec[:, 4:5]) if is0
                             else (lambda m=m: bin1[:, m:m + 1]))
                    mk("scalar", SC, lambda e, bank=bank, xb_ap=xb_ap:
                       e.activation(
                           out=xr[:, 3:3 + TS], in_=psum[bank][:, 0:TS],
                           func=AF.Identity, bias=xb_ap()))
                    release_bank(bank, "SC")
                    wait("vector", SC)
                    mk("vector", VC, lambda e, m=m: e.tensor_copy(
                        xr[:, 0:3], xrc[:, m * 3:(m + 1) * 3]), nop=True)
                    mk("vector", VC, lambda e, m=m: e.tensor_copy(
                        xrc[:, m * 3:(m + 1) * 3], xr[:, TS:TS + 3]), nop=True)
                    mk("vector", VC, lambda e, m=m: e.tensor_scalar(
                        out=cacc[0][:], in0=xr[:, 0:TS],
                        scalar1=convw[:, m * 4:m * 4 + 1], scalar2=None,
                        op0=ALU.mult))
                    for kk in range(1, 4):
                        mk("vector", VC, lambda e, m=m, kk=kk: e.scalar_tensor_tensor(
                            out=cacc[kk % 2][:], in0=xr[:, kk:kk + TS],
                            scalar=convw[:, m * 4 + kk:m * 4 + kk + 1],
                            in1=cacc[(kk + 1) % 2][:], op0=ALU.mult, op1=ALU.add))
                    wait("scalar", VC, TN, ST, LD)
                    wait_snap("scalar", REL[f"xc{par}"])
                    mk("scalar", SC, lambda e, m=m, par=par: e.activation(
                        out=xcr[par][:], in_=cacc[1][:], func=AF.Silu,
                        bias=convb[:, m:m + 1]))
                    wait("gpsimd", SC)
                    mk_dma("gpsimd", ST,
                           lambda m=m, c=c: spxc_h[m * 128:(m + 1) * 128,
                                                   c * TS:(c + 1) * TS],
                           lambda par=par: xcr[par][:])
                    if is0:
                        wait("tensor", SC, LD)
                        mk("tensor", TN, lambda e, m=m, par=par: e.matmul(
                            psum[3][:, 0:TS], wxbuf[:, m * 128:(m + 1) * 128],
                            xcr[par][:], start=(m == 0), stop=(m == NTI - 1)))
                else:
                    wait("scalar", TN, VC, ST)
                    zb_ap = ((lambda: cvec[:, 4:5]) if is0
                             else (lambda m=m: bin1[:, m:m + 1]))
                    mk("scalar", SC, lambda e, bank=bank, zb_ap=zb_ap:
                       e.activation(
                           out=zbuf[:], in_=psum[bank][:, 0:TS],
                           func=AF.Identity, bias=zb_ap()))
                    release_bank(bank, "SC")
                    wait("gpsimd", SC)
                    mk_dma("gpsimd", ST,
                           lambda m=m, c=c: spz_h[
                               (m - NTI) * 128:(m - NTI + 1) * 128,
                               c * TS:(c + 1) * TS],
                           lambda: zbuf[:])

            if is0:
                wait("vector", TN, ST)
                mk("vector", VC, lambda e: e.tensor_copy(
                    dtlr[0:64, :], psum[3][0:64, 0:TS]))
                mk("vector", VC, lambda e: e.tensor_copy(Bsb[:], psum[3][64:80, 0:TS]))
                mk("vector", VC, lambda e: e.tensor_copy(Csb[:], psum[3][96:112, 0:TS]))
                release_bank(3, "VC")
                wait("gpsimd", VC)
                mk_dma("gpsimd", ST, lambda c=c: spC_h[:, c * TS:(c + 1) * TS],
                       lambda: Csb[:])
                dtlrdy = cap(VC)
                build_rep(Bsb, Brep)
                build_rep(Csb, Crep)
                breprdy = cap(VC)
                for m in range(NTI):
                    par = m % 2
                    wait_snap("sync", REL[f"wd{par}"])
                    mk_dma("sync", LD, lambda par=par: wdtb[par][:],
                           lambda m=m: wdt0_h[m])
                    bank = m % 2
                    acquire_bank(bank)
                    wait("tensor", LD)
                    wait_snap("tensor", dtlrdy)
                    mk("tensor", TN, lambda e, bank=bank, par=par: e.matmul(
                        psum[bank][:, 0:TS], wdtb[par][:], dtlr[0:64, :],
                        start=True, stop=True))
                    REL[f"wd{par}"] = cap(TN)
                    softplus_from_psum(bank, bdt[:, m:m + 1], dtt[:])
                    scan_block(m, c)
            else:
                for gi, grp in enumerate(PROJ_GRPS):
                    nb = len(grp)
                    for b_ in range(nb):
                        acquire_bank(b_)
                    for k in range(NTI):
                        parx = k % 2
                        wait_snap("sync", REL[f"xc{parx}"])
                        mk_dma("sync", LD, lambda parx=parx: xcr[parx][:],
                               lambda k=k, c=c: spxc_h[k * 128:(k + 1) * 128,
                                                       c * TS:(c + 1) * TS])
                        wait_snap("sync", REL[f"wb{parx}"])
                        mk_dma("sync", LD,
                               lambda parx=parx, nb=nb: wbuf[parx][:, 0:nb * 128],
                               lambda k=k, grp=grp, nb=nb: wx1_h[
                                   k, grp[0]:grp[0] + nb].rearrange(
                                       "a p q -> p a q"))
                        wait("tensor", LD)
                        for bi in range(nb):
                            mk("tensor", TN,
                               lambda e, bi=bi, parx=parx, k=k: e.matmul(
                                   psum[bi][:, 0:TS],
                                   wbuf[parx][:, bi * 128:(bi + 1) * 128],
                                   xcr[parx][:],
                                   start=(k == 0), stop=(k == NTI - 1)))
                        REL[f"xc{parx}"] = cap(TN)
                        REL[f"wb{parx}"] = cap(TN)
                    for bi, m in enumerate(grp):
                        if m == 16:
                            wait("scalar", TN, ST, VC)
                            mk("scalar", SC, lambda e, bi=bi: e.activation(
                                out=Bsb[:], in_=psum[bi][0:NST, 0:TS],
                                func=AF.Identity, bias=bxbc[0:NST, 0:1]))
                            mk("scalar", SC, lambda e, bi=bi: e.activation(
                                out=Csb[:], in_=psum[bi][32:48, 0:TS],
                                func=AF.Identity, bias=bxbc[32:48, 0:1]))
                            release_bank(bi, "SC")
                        elif gi == 0:
                            softplus_from_psum(bi, bx1[:, m:m + 1],
                                               dtq[:, (m - 12) * TS:(m - 11) * TS])
                        else:
                            softplus_from_psum(bi, bx1[:, m:m + 1], dtt[:])
                            scan_block(m, c)
                    if gi == 0:
                        wait("gpsimd", SC)
                        mk_dma("gpsimd", ST,
                               lambda c=c: spC_h[:, c * TS:(c + 1) * TS],
                               lambda: Csb[:])
                        build_rep(Bsb, Brep)
                        build_rep(Csb, Crep)
                        for m in range(12, 16):
                            wait("vector", SC)
                            mk("vector", VC, lambda e, m=m: e.tensor_copy(
                                dtt[:], dtq[:, (m - 12) * TS:(m - 11) * TS]))
                            scan_block(m, c)


        wait("gpsimd", VC)
        mk_dma("gpsimd", ST, lambda: cc_in[l][:, 0:NTI], lambda: ccar[:])
        mk_dma("gpsimd", ST, lambda: cc_in[l][:, NTI:272], lambda: hcar[:])
        wait("gpsimd", ST)
        mk("gpsimd", GC, lambda e: e.collective_compute(
            "AllGather", ALU.bypass, replica_groups=GROUPS,
            ins=[cc_in[l]], outs=[cc_out[l]]))
        wait("sync", GC, VC, SC)
        mk_dma("sync", LD, lambda: gsb[:],
               lambda: cc_out[l].rearrange("j p f -> p j f"))

        mk("vector", VC, lambda e: e.memset(sssb[:], 0.0), nop=True)
        mk("vector", VC, lambda e: e.memset(hin[:], 0.0), nop=True)
        wait("vector", LD)
        for j in range(NCORES):
            for mm_ in range(NCORES):
                mk("vector", VC, lambda e, j=j, mm_=mm_: e.scalar_tensor_tensor(
                    out=sssb[:, j * NTI:(j + 1) * NTI],
                    in0=gsb[:, mm_ * 272:mm_ * 272 + NTI],
                    scalar=cmt[:, j * NCORES + mm_:j * NCORES + mm_ + 1],
                    in1=sssb[:, j * NTI:(j + 1) * NTI], op0=ALU.mult, op1=ALU.add), nop=True)
        for j in range(NCORES):
            for m in range(NTI):
                wait("scalar", VC)
                mk("scalar", SC, lambda e, j=j, m=m: e.activation(
                    out=etile[:], in_=asb[:, m * NST:(m + 1) * NST], func=AF.Exp,
                    scale=sssb[:, j * NTI + m:j * NTI + m + 1]))
                wait("vector", SC)
                mk("vector", VC, lambda e, j=j, m=m: e.tensor_mul(
                    ttile[:], etile[:],
                    gsb[:, j * 272 + NTI + m * NST:j * 272 + NTI + (m + 1) * NST]), nop=True)
                mk("vector", VC, lambda e, j=j, m=m: e.scalar_tensor_tensor(
                    out=hin[:, m * NST:(m + 1) * NST], in0=ttile[:],
                    scalar=predm[:, j:j + 1], in1=hin[:, m * NST:(m + 1) * NST],
                    op0=ALU.mult, op1=ALU.add), nop=True)

        for c in range(NCH):
            wait("sync", ST, VC, SC, TN)
            mk_dma("sync", LD, lambda: Csb[:],
                   lambda c=c: spC_h[:, c * TS:(c + 1) * TS])
            wait("tensor", LD)
            build_rep(Csb, Crep)
            if is0:
                ln_stats_to_xn(c, src_h, xn, TS)
            for m in range(NTI):
                wait("sync", VC, SC, ST)
                mk_dma("sync", LD, lambda: ysl[:],
                       lambda m=m, c=c: spy_h[m * 128:(m + 1) * 128,
                                              c * TS:(c + 1) * TS])
                mk_dma("sync", LD, lambda: cum[:],
                       lambda m=m, c=c: spcu_h[m * 128:(m + 1) * 128,
                                               c * TS:(c + 1) * TS])
                mk_dma("sync", LD, lambda: zbuf[:],
                       lambda m=m, c=c: spz_h[m * 128:(m + 1) * 128,
                                              c * TS:(c + 1) * TS])
                wait("scalar", LD, VC, ST)
                for n in range(NST):
                    mk("scalar", SC, lambda e, m=m, n=n: e.activation(
                        out=AR3[:, 0:TS], in_=cum[:], func=AF.Exp,
                        scale=asb[:, m * NST + n:m * NST + n + 1]))
                    wait("vector", SC)
                    mk("vector", VC, lambda e, n=n: e.tensor_mul(
                        AR3[:, TS:2 * TS], AR3[:, 0:TS],
                        Crep[:, n * TS:(n + 1) * TS]))
                    mk("vector", VC, lambda e, m=m, n=n: e.scalar_tensor_tensor(
                        out=ysl[:], in0=AR3[:, TS:2 * TS],
                        scalar=hin[:, m * NST + n:m * NST + n + 1],
                        in1=ysl[:], op0=ALU.mult, op1=ALU.add))
                    if n < NST - 1:
                        wait("scalar", VC)
                wait("scalar", LD, VC)
                mk("scalar", SC, lambda e: e.activation(
                    out=szb[:], in_=zbuf[:], func=AF.Silu))
                wait("vector", SC, TN)
                mk("vector", VC, lambda e, m=m: e.tensor_mul(
                    ygb[:, m * TS:(m + 1) * TS], ysl[:], szb[:]))
            ygrdy = cap(VC)
            for mo in range(NKD):
                wait_snap("sync", REL["wb0"])
                mk_dma("sync", LD, lambda: wbuf[0][:, 0:NTI * 128],
                       lambda mo=mo: wout_h[l][mo])
                bank = mo % 2
                acquire_bank(bank)
                if is0:
                    wait_snap("sync", REL["wb1"])
                    mk_dma("sync", LD, lambda: wbuf[1][:, 0:NKD * 128],
                           lambda mo=mo: wsel_h[:, mo * 1024:(mo + 1) * 1024])
                wait("tensor", LD)
                wait_snap("tensor", ygrdy)
                for k in range(NTI):
                    mk("tensor", TN, lambda e, bank=bank, k=k: e.matmul(
                        psum[bank][:, 0:TS], wbuf[0][:, k * 128:(k + 1) * 128],
                        ygb[:, k * TS:(k + 1) * TS],
                        start=(k == 0), stop=(k == NTI - 1)))
                REL["wb0"] = cap(TN)
                par = mo % 2
                wait("sync", VC, ST)
                mk_dma("sync", LD, lambda par=par: xa[par][:],
                       lambda mo=mo, c=c: src_h[mo * 128:(mo + 1) * 128,
                                                c * TS:(c + 1) * TS])
                if is0:
                    fbank = 4 + (mo % 2)
                    acquire_bank(fbank)
                    for k in range(NKD):
                        mk("tensor", TN, lambda e, fbank=fbank, k=k: e.matmul(
                            psum[fbank][:, 0:TS],
                            wbuf[1][:, k * 128:(k + 1) * 128],
                            xn[:, k * TS:(k + 1) * TS],
                            start=(k == 0), stop=(k == NKD - 1)))
                    REL["wb1"] = cap(TN)
                    wait("scalar", TN, VC)
                    mk("scalar", SC, lambda e, fbank=fbank, mo=mo: e.activation(
                        out=tmp8[:], in_=psum[fbank][:, 0:TS], func=AF.Identity,
                        bias=bsel[:, mo:mo + 1]))
                    release_bank(fbank, "SC")
                    wait("vector", TN, SC, LD, ST)
                    mk("vector", VC, lambda e, bank=bank, par=par: e.tensor_add(
                        scr[:], xa[par][:], psum[bank][:, 0:TS]))
                    release_bank(bank, "VC")
                    mk("vector", VC, lambda e: e.tensor_add(scr[:], scr[:], tmp8[:]))
                else:
                    wait("scalar", TN, VC)
                    mk("scalar", SC, lambda e, bank=bank, mo=mo: e.activation(
                        out=tmp8[:], in_=psum[bank][:, 0:TS], func=AF.Identity,
                        bias=bout1[:, mo:mo + 1]))
                    release_bank(bank, "SC")
                    wait("vector", SC, LD, ST, TN)
                    mk("vector", VC, lambda e, par=par: e.tensor_add(
                        scr[:], xa[par][:], tmp8[:]))
                if is0:
                    wait("gpsimd", VC)
                    mk_dma("gpsimd", ST,
                           lambda mo=mo, c=c: dst_h[mo * 128:(mo + 1) * 128,
                                                    c * TS:(c + 1) * TS],
                           lambda: scr[:])
                else:
                    for tt in range(TS // 128):
                        bank2 = 6 + (tt % 2)
                        acquire_bank(bank2)
                        wait("tensor", VC)
                        mk("tensor", TN, lambda e, bank2=bank2, tt=tt: e.transpose(
                            psum[bank2][:, 0:128], scr[:, tt * 128:(tt + 1) * 128],
                            eye[:]))
                        wait("vector", TN, ST)
                        mk("vector", VC, lambda e, bank2=bank2, tt=tt, mo=mo:
                           e.tensor_copy(
                               AR3[:, tt * 1024 + mo * 128:
                                   tt * 1024 + (mo + 1) * 128],
                               psum[bank2][:, 0:128]))
                        release_bank(bank2, "VC")
            if not is0:
                wait("gpsimd", VC)
                for tt in range(TS // 128):
                    mk_dma("gpsimd", ST,
                           lambda c=c, tt=tt: out_h[c * TS + tt * 128:
                                                    c * TS + (tt + 1) * 128, :],
                           lambda tt=tt: AR3[:, tt * 1024:(tt + 1) * 1024])

        if is0:
            wait("sync", ST, VC)
            mk_dma("sync", LD, lambda: xnh[:],
                   lambda: x1_h[:, TC - 3:TC].rearrange("(a p) t -> p a t", p=128))
            wait("gpsimd", LD)
            mk_dma("gpsimd", ST, lambda: cc2_in, lambda: xnh[:])
            wait("gpsimd", ST)
            mk("gpsimd", GC, lambda e: e.collective_compute(
                "AllGather", ALU.bypass, replica_groups=GROUPS,
                ins=[cc2_in], outs=[cc2_out]))
            wait("sync", GC, VC)
            mk_dma("sync", LD, lambda: gsb[:, 0:NCORES * 24],
                   lambda: cc2_out.rearrange("j p f -> p j f"))
            mk("vector", VC, lambda e: e.memset(g2sb[:], 0.0), nop=True)
            wait("vector", LD)
            for j in range(NCORES):
                mk("vector", VC, lambda e, j=j: e.scalar_tensor_tensor(
                    out=g2sb[:], in0=gsb[:, j * 24:(j + 1) * 24],
                    scalar=cmt[:, 56 + j:56 + j + 1],
                    in1=g2sb[:], op0=ALU.mult, op1=ALU.add), nop=True)

    def moe_phase():
        for c in range(NCH):
            ln_stats_to_xn(c, x1_h, xn, TS)
            wait("vector", ST)
            acquire_bank(0)
            wait("tensor", VC)
            for k in range(NKD):
                mk("tensor", TN, lambda e, k=k: e.matmul(
                    psum[0][0:NE, 0:TS], gwsb[:, k * 8:(k + 1) * 8],
                    xn[:, k * TS:(k + 1) * TS],
                    start=(k == 0), stop=(k == NKD - 1)))
            wait("vector", TN)
            mk("vector", VC, lambda e: e.tensor_copy(gfm[:], psum[0][0:NE, 0:TS]))
            release_bank(0, "VC")
            for tt in range(TS // 128):
                bank = 1 + (tt % 2)
                acquire_bank(bank)
                wait("tensor", VC)
                mk("tensor", TN, lambda e, bank=bank, tt=tt: e.transpose(
                    psum[bank][:, 0:NE], gfm[:, tt * 128:(tt + 1) * 128],
                    eye[0:NE, 0:NE]))
                wait("vector", TN)
                mk("vector", VC, lambda e, bank=bank, tt=tt: e.tensor_copy(
                    gtok[:, tt * NE:(tt + 1) * NE], psum[bank][:, 0:NE]), nop=True)
                release_bank(bank, "VC")
            NT4 = TS // 128
            wait("scalar", VC)
            for tt in range(NT4):
                mk("scalar", SC, lambda e, tt=tt: e.activation(
                    out=gtok[:, tt * NE:(tt + 1) * NE],
                    in_=gtok[:, tt * NE:(tt + 1) * NE], func=AF.Exp))
            wait("vector", SC)
            for tt in range(NT4):
                mk("vector", VC, lambda e, tt=tt: e.reduce_sum(
                    scr[:, tt:tt + 1], gtok[:, tt * NE:(tt + 1) * NE], axis=AX.X))
            for tt in range(NT4):
                mk("vector", VC, lambda e, tt=tt: e.reciprocal(
                    scr[:, 4 + tt:5 + tt], scr[:, tt:tt + 1]))
            for tt in range(NT4):
                mk("vector", VC, lambda e, tt=tt: e.tensor_scalar(
                    out=gtok[:, tt * NE:(tt + 1) * NE],
                    in0=gtok[:, tt * NE:(tt + 1) * NE],
                    scalar1=scr[:, 4 + tt:5 + tt], scalar2=None, op0=ALU.mult))
            for tt in range(NT4):
                mk("vector", VC, lambda e, tt=tt: e.reduce_max(
                    scr[:, 8 + tt:9 + tt], gtok[:, tt * NE:(tt + 1) * NE],
                    axis=AX.X))
            for tt in range(NT4):
                mk("vector", VC, lambda e, tt=tt: e.tensor_scalar(
                    out=tmp8[:, tt * NE:(tt + 1) * NE],
                    in0=gtok[:, tt * NE:(tt + 1) * NE],
                    scalar1=scr[:, 8 + tt:9 + tt], scalar2=None, op0=ALU.is_ge))
            for tt in range(NT4):
                mk("vector", VC, lambda e, tt=tt: e.scalar_tensor_tensor(
                    out=tmp8[:, 32 + tt * NE:32 + (tt + 1) * NE],
                    in0=tmp8[:, tt * NE:(tt + 1) * NE], scalar=-2.0,
                    in1=gtok[:, tt * NE:(tt + 1) * NE],
                    op0=ALU.mult, op1=ALU.add))
            for tt in range(NT4):
                mk("vector", VC, lambda e, tt=tt: e.reduce_max(
                    scr[:, 12 + tt:13 + tt],
                    tmp8[:, 32 + tt * NE:32 + (tt + 1) * NE], axis=AX.X))
            for tt in range(NT4):
                mk("vector", VC, lambda e, tt=tt: e.tensor_scalar(
                    out=tmp8[:, tt * NE:(tt + 1) * NE],
                    in0=gtok[:, tt * NE:(tt + 1) * NE],
                    scalar1=scr[:, 12 + tt:13 + tt], scalar2=None, op0=ALU.is_ge))
            for tt in range(NT4):
                mk("vector", VC, lambda e, c=c, tt=tt: e.tensor_copy(
                    selm[:, (c * 4 + tt) * NE:(c * 4 + tt + 1) * NE],
                    tmp8[:, tt * NE:(tt + 1) * NE]))
            for tt in range(NT4):
                mk("vector", VC, lambda e, tt=tt: e.tensor_add(
                    scr[:, 16 + tt:17 + tt], scr[:, 8 + tt:9 + tt],
                    scr[:, 12 + tt:13 + tt]))
            for tt in range(NT4):
                mk("vector", VC, lambda e, tt=tt: e.reciprocal(
                    scr[:, 20 + tt:21 + tt], scr[:, 16 + tt:17 + tt]))
            for tt in range(NT4):
                mk("vector", VC, lambda e, tt=tt: e.tensor_mul(
                    tmp8[:, tt * NE:(tt + 1) * NE], tmp8[:, tt * NE:(tt + 1) * NE],
                    gtok[:, tt * NE:(tt + 1) * NE]))
            for tt in range(NT4):
                mk("vector", VC, lambda e, tt=tt: e.tensor_scalar(
                    out=gtok[:, tt * NE:(tt + 1) * NE],
                    in0=tmp8[:, tt * NE:(tt + 1) * NE],
                    scalar1=scr[:, 20 + tt:21 + tt], scalar2=None, op0=ALU.mult),
                   nop=True)
            for tt in range(NT4):
                mk("vector", VC, lambda e, c=c, tt=tt: e.tensor_copy(
                    cmbm[:, (c * 4 + tt) * NE:(c * 4 + tt + 1) * NE],
                    gtok[:, tt * NE:(tt + 1) * NE]), nop=True)
            for tt in range(TS // 128):
                bank = 1 + (tt % 2)
                acquire_bank(bank)
                wait("tensor", VC)
                mk("tensor", TN, lambda e, bank=bank, tt=tt: e.transpose(
                    psum[bank][0:NE, 0:128], gtok[:, tt * NE:(tt + 1) * NE],
                    eye[:]))
                wait("vector", TN)
                mk("vector", VC, lambda e, bank=bank, tt=tt: e.tensor_copy(
                    gfm[:, tt * 128:(tt + 1) * 128], psum[bank][0:NE, 0:128]),
                   nop=True)
                release_bank(bank, "VC")
            for tt in range(TS // 128):
                for hh in range(2):
                    bank = 1 + hh
                    acquire_bank(bank)
                    wait("tensor", VC)
                    mk("tensor", TN, lambda e, bank=bank, tt=tt, hh=hh: e.matmul(
                        psum[bank][:, 0:TS], gfm[0:NE, tt * 128:(tt + 1) * 128],
                        b2sb[:, hh * 512:(hh + 1) * 512], start=True, stop=True))
                    wait("vector", TN, ST)
                    mk("vector", VC, lambda e, bank=bank, tt=tt, hh=hh: e.tensor_copy(
                        AR3[:, tt * 1024 + hh * 512:tt * 1024 + (hh + 1) * 512],
                        psum[bank][:, 0:TS]))
                    release_bank(bank, "VC")
            wait("gpsimd", VC, ST)
            for tt in range(TS // 128):
                mk_dma("gpsimd", ST,
                       lambda c=c, tt=tt: out_h[c * TS + tt * 128:
                                                c * TS + (tt + 1) * 128, :],
                       lambda tt=tt: AR3[:, tt * 1024:(tt + 1) * 1024], accum=True)
            for tt in range(TS // 128):
                for kd in range(NKD):
                    bank = 6 + (kd % 2)
                    acquire_bank(bank)
                    wait("tensor", VC)
                    mk("tensor", TN, lambda e, bank=bank, tt=tt, kd=kd: e.transpose(
                        psum[bank][:, 0:128],
                        xn[:, kd * TS + tt * 128:kd * TS + (tt + 1) * 128], eye[:]))
                    wait("vector", TN, ST)
                    mk("vector", VC, lambda e, bank=bank, kd=kd: e.tensor_copy(
                        AR2[:, kd * 128:(kd + 1) * 128], psum[bank][:, 0:128]))
                    release_bank(bank, "VC")
                wait("gpsimd", VC)
                mk_dma("gpsimd", ST,
                       lambda c=c, tt=tt: xnt_h[c * TS + tt * 128:
                                                c * TS + (tt + 1) * 128, :],
                       lambda: AR2[:, 0:1024])

        for m in range(NTI):
            par = m % 2
            wait_snap("sync", REL[f"wb{par}"])
            mk_dma("sync", LD, lambda par=par: wbuf[par][:, 0:2048],
                   lambda m=m: ltri_h[m])
            bank = m % 2
            acquire_bank(bank)
            wait("tensor", LD, VC)
            for k in range(NTI):
                mk("tensor", TN, lambda e, bank=bank, par=par, k=k: e.matmul(
                    psum[bank][:, 0:NE], wbuf[par][:, k * 128:(k + 1) * 128],
                    selm[:, k * NE:(k + 1) * NE],
                    start=(k == 0), stop=(k == NTI - 1)))
            REL[f"wb{par}"] = cap(TN)
            wait("vector", TN)
            mk("vector", VC, lambda e, bank=bank, m=m: e.tensor_copy(
                rankm[:, m * NE:(m + 1) * NE], psum[bank][:, 0:NE]), nop=True)
            release_bank(bank, "VC")
        acquire_bank(2)
        wait("tensor", VC)
        for k in range(NTI):
            mk("tensor", TN, lambda e, k=k: e.matmul(
                psum[2][0:1, 0:NE], onesc[:], selm[:, k * NE:(k + 1) * NE],
                start=(k == 0), stop=(k == NTI - 1)))
        wait("vector", TN)
        mk("vector", VC, lambda e: e.tensor_copy(
            scr[0:1, 0:NE], psum[2][0:1, 0:NE]), nop=True)
        release_bank(2, "VC")
        mk("vector", VC, lambda e: e.tensor_copy(itmp[0:1, 0:NE], scr[0:1, 0:NE]),
           nop=True)
        mk("vector", VC, lambda e: e.tensor_scalar(
            out=itmp[0:1, 8:8 + NE], in0=itmp[0:1, 0:NE], scalar1=TSLOT - 1,
            scalar2=None, op0=ALU.add), nop=True)
        mk("vector", VC, lambda e: e.tensor_scalar(
            out=itmp[0:1, 8:8 + NE], in0=itmp[0:1, 8:8 + NE], scalar1=8,
            scalar2=None, op0=ALU.arith_shift_right), nop=True)
        mk("vector", VC, lambda e: e.tensor_scalar(
            out=itmp[0:1, 8:8 + NE], in0=itmp[0:1, 8:8 + NE], scalar1=8,
            scalar2=None, op0=ALU.arith_shift_left), nop=True)
        mk("vector", VC, lambda e: e.tensor_copy(
            scr[0:1, 8:8 + NE], itmp[0:1, 8:8 + NE]), nop=True)
        mk("vector", VC, lambda e: e.memset(scr[0:1, 16:24], 0.0))
        mk("vector", VC, lambda e: e.engine_nop())
        for e_ in range(1, NE):
            mk("vector", VC, lambda e, e_=e_: e.tensor_add(
                scr[0:1, 16 + e_:17 + e_], scr[0:1, 15 + e_:16 + e_],
                scr[0:1, 7 + e_:8 + e_]), nop=True)
        acquire_bank(3)
        wait("tensor", VC)
        mk("tensor", TN, lambda e: e.matmul(
            psum[3][:, 0:NE], onesr[:], scr[0:1, 16:24], start=True, stop=True))
        wait("vector", TN)
        mk("vector", VC, lambda e: e.tensor_copy(masksb[:], psum[3][:, 0:NE]),
           nop=True)
        release_bank(3, "VC")
        for sl_ in range(NTI):
            mk("vector", VC, lambda e, sl_=sl_: e.tensor_add(
                rankm[:, sl_ * NE:(sl_ + 1) * NE],
                rankm[:, sl_ * NE:(sl_ + 1) * NE], masksb[:]), nop=True)
        for sl_ in range(NTI):
            mk("vector", VC, lambda e, sl_=sl_: e.tensor_scalar(
                out=rankm[:, sl_ * NE:(sl_ + 1) * NE],
                in0=rankm[:, sl_ * NE:(sl_ + 1) * NE], scalar1=float(-SDUMP),
                scalar2=None, op0=ALU.add), nop=True)
        for sl_ in range(NTI):
            mk("vector", VC, lambda e, sl_=sl_: e.tensor_mul(
                rankm[:, sl_ * NE:(sl_ + 1) * NE],
                rankm[:, sl_ * NE:(sl_ + 1) * NE],
                selm[:, sl_ * NE:(sl_ + 1) * NE]), nop=True)
        for sl_ in range(NTI):
            mk("vector", VC, lambda e, sl_=sl_: e.tensor_scalar(
                out=rankm[:, sl_ * NE:(sl_ + 1) * NE],
                in0=rankm[:, sl_ * NE:(sl_ + 1) * NE], scalar1=float(SDUMP),
                scalar2=None, op0=ALU.add), nop=True)
        mk("vector", VC, lambda e: e.tensor_copy(posi[:], rankm[:]), nop=True)
        mk("vector", VC, lambda e: e.memset(scr[0:1, 32:32 + NTILES], 0.0))
        mk("vector", VC, lambda e: e.engine_nop())
        for e_ in range(NE):
            mk("vector", VC, lambda e, e_=e_: e.tensor_scalar(
                out=scr[0:1, 64:64 + NTILES], in0=tstart[0:1, 0:NTILES],
                scalar1=scr[0:1, 16 + e_:17 + e_], scalar2=None, op0=ALU.is_ge),
               nop=True)
            mk("vector", VC, lambda e: e.tensor_add(
                scr[0:1, 32:32 + NTILES], scr[0:1, 32:32 + NTILES],
                scr[0:1, 64:64 + NTILES]), nop=True)
        mk("vector", VC, lambda e: e.tensor_scalar(
            out=scr[0:1, 32:32 + NTILES], in0=scr[0:1, 32:32 + NTILES],
            scalar1=-1.0, scalar2=None, op0=ALU.add), nop=True)
        acquire_bank(3)
        wait("tensor", VC)
        mk("tensor", TN, lambda e: e.matmul(
            psum[3][:, 0:NTILES], onesr[:], scr[0:1, 32:32 + NTILES],
            start=True, stop=True))
        wait("vector", TN)
        mk("vector", VC, lambda e: e.tensor_copy(
            exall[:, 0:NTILES], psum[3][:, 0:NTILES]), nop=True)
        release_bank(3, "VC")
        mk("vector", VC, lambda e: e.memset(tmp8[:, 0:STBL // 128], 0.0))
        wait("gpsimd", VC)
        mk_dma("gpsimd", ST, lambda: stok_h, lambda: tmp8[:, 0:STBL // 128])
        mk_dma("gpsimd", ST, lambda: scmb_h, lambda: tmp8[:, 0:STBL // 128])
        wait("gpsimd", ST)
        for sl_ in range(NTI):
            for e_ in range(NE):
                def scf1(e, sl_=sl_, e_=e_):
                    e.indirect_dma_start(
                        out=stok_h[0:128, 0:1],
                        out_offset=IndirectOffsetOnAxis(
                            posi[:, sl_ * NE + e_:sl_ * NE + e_ + 1], 0),
                        in_=tokid[:, sl_:sl_ + 1], in_offset=None,
                    ).then_inc(ST.h, 16)
                emit("gpsimd", scf1)
                ST.n += 16
                def scf2(e, sl_=sl_, e_=e_):
                    e.indirect_dma_start(
                        out=scmb_h[0:128, 0:1],
                        out_offset=IndirectOffsetOnAxis(
                            posi[:, sl_ * NE + e_:sl_ * NE + e_ + 1], 0),
                        in_=cmbm[:, sl_ * NE + e_:sl_ * NE + e_ + 1],
                        in_offset=None,
                    ).then_inc(ST.h, 16)
                emit("gpsimd", scf2)
                ST.n += 16
        wait("gpsimd", ST)

        for i in range(NTILES):
            wait("vector", ST)
            mk("vector", VC, lambda e, i=i: e.tensor_scalar(
                out=tkf[:, 0:1], in0=exall[:, i:i + 1], scalar1=4096.0,
                scalar2=None, op0=ALU.mult), nop=True)
            mk("vector", VC, lambda e: e.tensor_add(
                tkf[:, 0:1], tkf[:, 0:1], iotap[:]), nop=True)
            mk("vector", VC, lambda e: e.tensor_copy(
                idxi[:, 0:1], tkf[:, 0:1]), nop=True)
            mk("vector", VC, lambda e, i=i: e.tensor_scalar(
                out=tkf[:, 1:2], in0=exall[:, i:i + 1], scalar1=2048.0,
                scalar2=None, op0=ALU.mult), nop=True)
            mk("vector", VC, lambda e: e.tensor_add(
                tkf[:, 1:2], tkf[:, 1:2], iotap[:]), nop=True)
            mk("vector", VC, lambda e: e.tensor_copy(
                idxi[:, 1:2], tkf[:, 1:2]), nop=True)
            mk("vector", VC, lambda e, i=i: e.tensor_scalar(
                out=tkf[:, 0:1], in0=exall[:, i:i + 1], scalar1=32.0,
                scalar2=None, op0=ALU.mult), nop=True)
            mk("vector", VC, lambda e: e.tensor_add(
                tkf[0:32, 0:1], tkf[0:32, 0:1], iotap[0:32, 0:1]), nop=True)
            mk("vector", VC, lambda e: e.tensor_copy(
                idxi[0:32, 4:5], tkf[0:32, 0:1]), nop=True)
            wait("gpsimd", VC)
            def b1fn(e):
                e.indirect_dma_start(
                    out=b1g[:, :], out_offset=None,
                    in_=b1t_h[0:32, :],
                    in_offset=IndirectOffsetOnAxis(idxi[0:32, 4:5], 0),
                ).then_inc(ST.h, 16)
            emit("gpsimd", b1fn)
            ST.n += 16
            acquire_bank(5)
            wait("tensor", ST)
            mk("tensor", TN, lambda e: e.transpose(
                psum[5][:, 0:32], b1g[:, :], eye[0:32, 0:32]))
            wait("vector", TN, SC)
            mk("vector", VC, lambda e: e.tensor_copy(
                b1x[:], psum[5][:, 0:32]), nop=True)
            release_bank(5, "VC")
            wait("sync", VC, TN)
            for hh in range(2):
                mk_dma("sync", LD, lambda hh=hh: tkf[:, hh:hh + 1],
                       lambda i=i, hh=hh: stok_h[i * TSLOT + hh * 128:
                                                 i * TSLOT + (hh + 1) * 128, 0:1])
            mk_dma("sync", LD, lambda: scr[0:1, 256:256 + TSLOT],
                   lambda i=i: scmb_h[i * TSLOT:(i + 1) * TSLOT, 0:1])
            wait("vector", LD)
            for hh in range(2):
                mk("vector", VC, lambda e, hh=hh: e.tensor_copy(
                    idxi[:, 2 + hh:3 + hh], tkf[:, hh:hh + 1]), nop=True)
            wait("gpsimd", VC)
            for hh in range(2):
                mk_dma_indirect_gather = None
                def gfn(e, hh=hh):
                    e.indirect_dma_start(
                        out=AR3[:, 2048 + hh * 1024:2048 + (hh + 1) * 1024],
                        out_offset=None,
                        in_=xnt_h[0:128, :],
                        in_offset=IndirectOffsetOnAxis(idxi[:, 2 + hh:3 + hh], 0),
                    ).then_inc(ST.h, 16)
                emit("gpsimd", gfn)
                ST.n += 16
            for hh in range(2):
                for kd in range(NKD):
                    bank = 6 + (kd % 2)
                    acquire_bank(bank)
                    wait("tensor", ST, VC)
                    mk("tensor", TN, lambda e, bank=bank, hh=hh, kd=kd: e.transpose(
                        psum[bank][:, 0:128],
                        AR3[:, 2048 + hh * 1024 + kd * 128:
                            2048 + hh * 1024 + (kd + 1) * 128],
                        eye[:]))
                    wait("vector", TN)
                    mk("vector", VC, lambda e, bank=bank, hh=hh, kd=kd: e.tensor_copy(
                        AR3[:, 4096 + kd * TSLOT + hh * 128:
                            4096 + kd * TSLOT + (hh + 1) * 128],
                        psum[bank][:, 0:128]))
                    release_bank(bank, "VC")
            acquire_bank(6)
            wait("tensor", VC, LD)
            mk("tensor", TN, lambda e: e.matmul(
                psum[6][:, 0:TSLOT], onesr[:], scr[0:1, 256:256 + TSLOT],
                start=True, stop=True))
            wait("vector", TN)
            mk("vector", VC, lambda e: e.tensor_copy(
                AR2[:, 2048:2048 + TSLOT], psum[6][:, 0:TSLOT]))
            release_bank(6, "VC")
            xgrdy = cap(VC)
            for m in range(32):
                par = m % 2
                wait_snap("gpsimd", REL[f"wb{par}"])
                wait("gpsimd", VC)

                def wfn(e, par=par, m=m):
                    e.indirect_dma_start(
                        out=wbuf[par][:, 0:NKD * 128], out_offset=None,
                        in_=w1_h[0, 0],
                        in_offset=IndirectOffsetOnAxis(idxi[:, 0:1], 0),
                        element_offset=m * 128 * 1024,
                    ).then_inc(ST.h, 16)
                emit("gpsimd", wfn)
                ST.n += 16
                bank = m % 4
                acquire_bank(bank)
                wait("tensor", ST)
                wait_snap("tensor", xgrdy)
                for k in range(NKD):
                    mk("tensor", TN, lambda e, bank=bank, par=par, k=k: e.matmul(
                        psum[bank][:, 0:TSLOT],
                        wbuf[par][:, k * 128:(k + 1) * 128],
                        AR3[:, 4096 + k * TSLOT:4096 + (k + 1) * TSLOT],
                        start=(k == 0), stop=(k == NKD - 1)))
                REL[f"wb{par}"] = cap(TN)
                wait("scalar", TN)
                mk("scalar", SC, lambda e, bank=bank, m=m: e.activation(
                    out=hsb[:, m * TSLOT:(m + 1) * TSLOT],
                    in_=psum[bank][:, 0:TSLOT], func=AF.Relu,
                    bias=b1x[:, m:m + 1]))
                release_bank(bank, "SC")
            hrdy = cap(SC)
            for mo in range(NKD):
                wait_snap("gpsimd", REL["wb0"])
                wait_snap("gpsimd", REL["wb1"])

                def w2fn(e, mo=mo):
                    e.indirect_dma_start(
                        out=wbuf[0][:, 0:2048], out_offset=None,
                        in_=w2_h[0, 0, 0],
                        in_offset=IndirectOffsetOnAxis(idxi[:, 1:2], 0),
                        element_offset=mo * 2 * 128 * 2048,
                    ).then_inc(ST.h, 16)
                emit("gpsimd", w2fn)
                ST.n += 16

                def w2fn2(e, mo=mo):
                    e.indirect_dma_start(
                        out=wbuf[1][:, 0:2048], out_offset=None,
                        in_=w2_h[0, 0, 0],
                        in_offset=IndirectOffsetOnAxis(idxi[:, 1:2], 0),
                        element_offset=(mo * 2 + 1) * 128 * 2048,
                    ).then_inc(ST.h, 16)
                emit("gpsimd", w2fn2)
                ST.n += 16
                bank = 4 + (mo % 2)
                acquire_bank(bank)
                wait("tensor", ST)
                wait_snap("tensor", hrdy)
                for k in range(32):
                    kh, kk = k // 16, k % 16
                    mk("tensor", TN, lambda e, bank=bank, kh=kh, kk=kk, k=k: e.matmul(
                        psum[bank][:, 0:TSLOT],
                        wbuf[kh][:, kk * 128:(kk + 1) * 128],
                        hsb[:, k * TSLOT:(k + 1) * TSLOT],
                        start=(k == 0), stop=(k == 31)))
                REL["wb0"] = cap(TN)
                REL["wb1"] = cap(TN)
                wait("vector", TN, ST)
                mk("vector", VC, lambda e, bank=bank, mo=mo: e.tensor_mul(
                    AR2[:, mo * TSLOT:(mo + 1) * TSLOT], psum[bank][:, 0:TSLOT],
                    AR2[:, 2048:2048 + TSLOT]))
                release_bank(bank, "VC")
            for hh in range(2):
                for kd in range(NKD):
                    bank = 6 + (kd % 2)
                    acquire_bank(bank)
                    wait("tensor", VC)
                    mk("tensor", TN, lambda e, bank=bank, hh=hh, kd=kd: e.transpose(
                        psum[bank][:, 0:128],
                        AR2[:, kd * TSLOT + hh * 128:kd * TSLOT + (hh + 1) * 128],
                        eye[:]))
                    wait("vector", TN, ST)
                    mk("vector", VC, lambda e, bank=bank, hh=hh, kd=kd: e.tensor_copy(
                        AR3[:, hh * 1024 + kd * 128:hh * 1024 + (kd + 1) * 128],
                        psum[bank][:, 0:128]))
                    release_bank(bank, "VC")
            wait("gpsimd", VC, ST)
            for hh in range(2):
                def sfn(e, hh=hh):
                    e.indirect_dma_start(
                        out=out_h[0:128, :],
                        out_offset=IndirectOffsetOnAxis(idxi[:, 2 + hh:3 + hh], 0),
                        in_=AR3[:, hh * 1024:(hh + 1) * 1024], in_offset=None,
                        compute_op=ALU.add,
                    ).then_inc(ST.h, 16)
                emit("gpsimd", sfn)
                ST.n += 16

    layer(0)
    layer(1)
    moe_phase()
    wait("gpsimd", ST, GC)
    wait("sync", ST)

    with nc.Block() as block:
        @block.sync
        def _(eng):
            for e_, fn in plan:
                if e_ == "sync":
                    fn(eng)

        @block.tensor
        def _(eng):
            for e_, fn in plan:
                if e_ == "tensor":
                    fn(eng)

        @block.vector
        def _(eng):
            for e_, fn in plan:
                if e_ == "vector":
                    fn(eng)

        @block.scalar
        def _(eng):
            for e_, fn in plan:
                if e_ == "scalar":
                    fn(eng)

        @block.gpsimd
        def _(eng):
            for e_, fn in plan:
                if e_ == "gpsimd":
                    fn(eng)

    ctx.close()
    return nc


def _tile_lhsT(w, n_m=None):
    Ktot, Mtot = w.shape
    n_k = Ktot // 128
    if n_m is None:
        n_m = Mtot // 128
    out = np.empty((n_m, 128, n_k * 128), np.float32)
    for m in range(n_m):
        blk = w[:, m * 128:(m + 1) * 128].reshape(n_k, 128, 128)
        out[m] = blk.transpose(1, 0, 2).reshape(128, n_k * 128)
    return np.ascontiguousarray(out)


def _col128(v):
    return np.ascontiguousarray(v.reshape(-1, 128).T.astype(np.float32))


def prepare_in_maps(inp):
    f = lambda k: np.asarray(inp[k], np.float32)
    x = f("L0_attn_x")
    B_, L_, _ = x.shape

    win0 = _tile_lhsT(f("L0_attn_W_in"))
    wx0p = np.zeros((DI, 128), np.float32)
    wx0p[:, 0:64] = f("L0_attn_W_x")[:, 0:64]
    wx0p[:, 64:80] = f("L0_attn_W_x")[:, 64:80]
    wx0p[:, 96:112] = f("L0_attn_W_x")[:, 80:96]
    wx0_img = np.ascontiguousarray(
        wx0p.reshape(NTI, 128, 128).transpose(1, 0, 2).reshape(128, NTI * 128))
    wdt0 = np.ascontiguousarray(
        f("L0_attn_W_dt").reshape(64, NTI, 128).transpose(1, 0, 2))
    wout0 = _tile_lhsT(f("L0_attn_W_out"))
    win1 = _tile_lhsT(np.ascontiguousarray(f("L1_attn_W_in").T))
    wx1t = np.ascontiguousarray(f("L1_attn_W_x").T)
    wx1m = np.zeros((DI, 17 * 128), np.float32)
    wx1m[:, 0:2048] = wx1t[:, 0:2048]
    wx1m[:, 2048:2064] = wx1t[:, 2048:2064]
    wx1m[:, 2048 + 32:2048 + 48] = wx1t[:, 2064:2080]
    wx1 = np.ascontiguousarray(
        wx1m.reshape(NTI, 128, 17, 128).transpose(0, 2, 1, 3))
    wout1 = _tile_lhsT(np.ascontiguousarray(f("L1_attn_W_out").T))
    ffnw = np.stack([_tile_lhsT(f("L0_ffn_W")[e]) for e in range(NE)])
    w1 = np.stack([_tile_lhsT(np.ascontiguousarray(f("L1_ffn_w1")[e].T))
                   for e in range(NE)])
    w2full = np.stack([_tile_lhsT(np.ascontiguousarray(f("L1_ffn_w2")[e].T))
                       for e in range(NE)])
    w2 = np.ascontiguousarray(
        w2full.reshape(NE, 8, 128, 2, 16 * 128).transpose(0, 1, 3, 2, 4))
    gw = np.ascontiguousarray(f("L1_ffn_gate_w").T)
    gatew = np.ascontiguousarray(
        gw.reshape(NKD, 128, 8).transpose(1, 0, 2).reshape(128, NKD * 8))
    bcs = np.zeros((NST, NST * 128), np.float32)
    for n in range(NST):
        bcs[n, n * 128:(n + 1) * 128] = 1.0

    shared = {
        "win0": win0, "win1": win1, "wx0": wx0_img, "wdt0": wdt0, "wx1": wx1,
        "wout0": wout0, "wout1": wout1,
        "convw0": np.ascontiguousarray(
            f("L0_attn_conv_w").reshape(NTI, 128, 4).transpose(1, 0, 2)
            .reshape(128, -1)),
        "convw1": np.ascontiguousarray(
            f("L1_attn_W_conv").reshape(DI, 4).reshape(NTI, 128, 4)
            .transpose(1, 0, 2).reshape(128, -1)),
        "convb0": _col128(f("L0_attn_conv_b")),
        "convb1": _col128(f("L1_attn_b_conv")),
        "alog0": np.ascontiguousarray(
            f("L0_attn_A_log").reshape(NTI, 128, NST).transpose(1, 0, 2)
            .reshape(128, -1)),
        "alog1": np.ascontiguousarray(
            f("L1_attn_A_log").reshape(NTI, 128, NST).transpose(1, 0, 2)
            .reshape(128, -1)),
        "dskip0": _col128(f("L0_attn_D_skip")),
        "dskip1": _col128(f("L1_attn_D")),
        "bdt0": _col128(f("L0_attn_b_dt")),
        "bin1": np.ascontiguousarray(f("L1_attn_b_in").reshape(32, 128).T),
        "bx1": np.ascontiguousarray(
            np.concatenate([f("L1_attn_b_x")[:DI].reshape(16, 128).T,
                            np.zeros((128, 1), np.float32)], axis=1)),
        "bxbc1": (lambda bb: (bb.__setitem__((slice(0, 16), 0),
                                              f("L1_attn_b_x")[DI:DI + 16]),
                              bb.__setitem__((slice(32, 48), 0),
                                              f("L1_attn_b_x")[DI + 16:DI + 32]),
                              bb)[-1])(np.zeros((128, 1), np.float32)),
        "bout1": np.ascontiguousarray(f("L1_attn_b_out").reshape(8, 128).T),
        "ffnw": ffnw,
        "ffnb": np.ascontiguousarray(
            f("L0_ffn_b").reshape(NE, 8, 128).transpose(2, 0, 1).reshape(128, -1)),
        "logits": f("L0_ffn_logits").reshape(1, NE),
        "gumu": f("L0_ffn_u").reshape(1, NE),
        "w1": w1, "w2": w2,
        "b1": np.ascontiguousarray(
            f("L1_ffn_b1").reshape(NE, 32, 128).transpose(2, 0, 1).reshape(128, -1)),
        "b2": f("L1_ffn_b2"),
        "gatew": gatew,
        "bcsel": bcs,
        "ltri": _tile_lhsT(
            (np.arange(TC)[:, None] < np.arange(TC)[None, :]).astype(np.float32)),
        "iota": np.arange(128, dtype=np.float32).reshape(128, 1),
        "tokid": np.ascontiguousarray(
            (np.arange(16)[None, :] * 128
             + np.arange(128)[:, None]).astype(np.float32)),
        "tstart": (np.arange(32, dtype=np.float32) * TSLOT).reshape(1, 32),
        "b1t": np.ascontiguousarray(f("L1_ffn_b1").reshape(NE * 32, 128)),
        "eye": np.eye(128, dtype=np.float32),
        "cvec": np.ascontiguousarray(
            np.tile(np.array([[EPS, -0.5, -1.0, 1.0, 0.0]], np.float32), (128, 1))),
    }

    in_maps = []
    for c in range(NCORES):
        b, q = c // 4, c % 4
        x0 = np.ascontiguousarray(x[b, q * TC:(q + 1) * TC, :].T)
        if q == 0:
            x0e = np.zeros((D, 3), np.float32)
        else:
            x0e = np.ascontiguousarray(x[b, q * TC - 3:q * TC, :].T)
        pred = np.zeros((128, NCORES), np.float32)
        cm = np.zeros((NCORES, NCORES), np.float32)
        for j in range(NCORES):
            if j // 4 == b and (j % 4) < q:
                pred[:, j] = 1.0
                for m_ in range(j + 1, c):
                    if m_ // 4 == b:
                        cm[j, m_] = 1.0
        cm[7, :] = 0.0
        cm[7, (c - 1) % NCORES] = 1.0 if q > 0 else 0.0
        cmt = np.ascontiguousarray(
            np.broadcast_to(cm.reshape(1, -1),
                            (128, NCORES * NCORES)).astype(np.float32))
        halom = np.full((128, 1), 0.0 if q == 0 else 1.0, np.float32)
        m = {"x0": x0, "x0ext": x0e, "predm": pred, "cmt": cmt, "halom": halom}
        m.update(shared)
        in_maps.append(m)
    return in_maps, (B_, L_)


def kernel(**inp):
    in_maps, (B_, L_) = prepare_in_maps(inp)
    nc = build_program()
    res = run_bass_kernel_spmd(nc, in_maps, core_ids=list(range(NCORES)))
    out = np.empty((B_, L_, D), np.float32)
    for c in range(NCORES):
        b, q = c // 4, c % 4
        out[b, q * TC:(q + 1) * TC, :] = res.results[c]["o"]
    return out
